# version 04 (final)
# baseline (speedup 1.0000x reference)
import contextlib
import numpy as np
import ml_dtypes
import concourse.bass as bass
import concourse.mybir as mybir
from concourse.bass_utils import run_bass_kernel_spmd

F32 = mybir.dt.float32
BF16 = mybir.dt.bfloat16
F16 = mybir.dt.float16
AF = mybir.ActivationFunctionType
ALU = mybir.AluOpType
AX = mybir.AxisListType
BFNP = ml_dtypes.bfloat16

NCORES = 8
D = 2048
HID = 5632
TOK = 1024
DCN = 16
NH = 16
NCH = 16
QSCALE = float(1.0 / np.sqrt(128.0))
NW = float(HID * D)
INV_D = 1.0 / 2048.0

SEMS = ("dc0", "dx", "dgm", "dpa", "dpb", "dsa", "dsb", "dla", "dlb",
        "dwa", "dwb", "d3a", "d3b", "dqa", "dqb",
        "t", "v", "s", "st",
        "car", "cq1", "cq2", "cq3", "cwq", "cwk", "cwv", "cwo", "cltg", "cltg2")


class Prog:
    def __init__(self, nc, es):
        self.nc = nc
        self.ops = {k: [] for k in ("sync", "tensor", "vector", "scalar", "gpsimd")}
        self.sems = {n: es.enter_context(nc.semaphore(n)) for n in SEMS}
        self.cnt = {n: 0 for n in SEMS}

    def emit(self, eng, fn, inc=None, n=1):
        if inc is None:
            self.ops[eng].append(fn)
            return None
        if self.cnt[inc] > 0 and eng not in ("sync", "gpsimd"):
            self.wait(eng, inc, self.cnt[inc])
        self.cnt[inc] += n
        s = self.sems[inc]
        self.ops[eng].append(lambda e, fn=fn, s=s, n=n: fn(e).then_inc(s, n))
        return self.cnt[inc]

    def wait(self, eng, inc, val):
        if not val or val <= 0:
            return
        s = self.sems[inc]
        self.ops[eng].append(lambda e, s=s, v=val: e.wait_ge(s, v))

    def val(self, inc):
        return self.cnt[inc]


def DMA(p, q, sem, out, in_):
    return p.emit(q, lambda e, o=out, i=in_: e.dma_start(out=o, in_=i), inc=sem, n=16)


def MM(p, out, lhsT, rhs, start, stop, inc=None):
    return p.emit("tensor",
                  lambda e, o=out, l=lhsT, r=rhs, a=start, b=stop: e.matmul(o, l, r, start=a, stop=b),
                  inc=inc)


def build(upto="all"):
    nc = bass.Bass("TRN2", target_bir_lowering=False, debug=False)

    xt = nc.dram_tensor("xt", [D, TOK], F32, kind="ExternalInput").ap()
    w1t_sh = nc.dram_tensor("w1t_sh", [256, HID], F32, kind="ExternalInput").ap()
    w2t_sh = nc.dram_tensor("w2t_sh", [256, HID], F32, kind="ExternalInput").ap()
    w3t_sh = nc.dram_tensor("w3t_sh", [704, D], F32, kind="ExternalInput").ap()
    wqt_sh = nc.dram_tensor("wqt_sh", [256, D], F32, kind="ExternalInput").ap()
    wkt_sh = nc.dram_tensor("wkt_sh", [256, D], F32, kind="ExternalInput").ap()
    wvt_sh = nc.dram_tensor("wvt_sh", [256, D], F32, kind="ExternalInput").ap()
    wot_sh = nc.dram_tensor("wot_sh", [256, D], F32, kind="ExternalInput").ap()
    ident_i = nc.dram_tensor("ident_i", [128, 128], BF16, kind="ExternalInput").ap()
    maskt_i = nc.dram_tensor("maskt_i", [128, 128], F32, kind="ExternalInput").ap()
    onesb_i = nc.dram_tensor("onesb_i", [128, 1], BF16, kind="ExternalInput").ap()
    onescf_i = nc.dram_tensor("onescf_i", [128, 1], F32, kind="ExternalInput").ap()
    onesrf_i = nc.dram_tensor("onesrf_i", [1, 128], F32, kind="ExternalInput").ap()
    cb_i = nc.dram_tensor("cb_i", [128, 8], F32, kind="ExternalInput").ap()
    selv_i = nc.dram_tensor("selv_i", [1, 8], F32, kind="ExternalInput").ap()

    yt = nc.dram_tensor("yt", [D, TOK], F32, kind="ExternalOutput").ap()

    gsrc = nc.dram_tensor("gsrc", [1, 8], F32)
    gdst = nc.dram_tensor("gdst", [1, 8], F32)
    q1loc = nc.dram_tensor("q1loc", [256, HID], BF16)
    q2loc = nc.dram_tensor("q2loc", [256, HID], BF16)
    q3loc = nc.dram_tensor("q3loc", [704, D], BF16)
    wqloc = nc.dram_tensor("wqloc", [256, D], BF16)
    wkloc = nc.dram_tensor("wkloc", [256, D], BF16)
    wvloc = nc.dram_tensor("wvloc", [256, D], BF16)
    woloc = nc.dram_tensor("woloc", [256, D], BF16)
    q1t = nc.dram_tensor("q1t", [D, HID], BF16)
    q2t = nc.dram_tensor("q2t", [D, HID], BF16)
    q3t = nc.dram_tensor("q3t", [HID, D], BF16)
    wqt = nc.dram_tensor("wqt", [D, D], BF16)
    wkt = nc.dram_tensor("wkt", [D, D], BF16)
    wvt = nc.dram_tensor("wvt", [D, D], BF16)
    wot = nc.dram_tensor("wot", [D, D], BF16)
    ltoc_lo = nc.dram_tensor("ltoc_lo", [128, 1024], F16)
    ltoc_hi = nc.dram_tensor("ltoc_hi", [128, 1024], F16)
    ltg_lo = nc.dram_tensor("ltg_lo", [128 * NCORES, 1024], F16)
    ltg_hi = nc.dram_tensor("ltg_hi", [128 * NCORES, 1024], F16)

    es = contextlib.ExitStack()
    with es:
        A = nc.alloc_sbuf_tensor_at
        stream = A("stream", [128, 16 * 1024], F32, offset=16384)
        xn = A("xn", [128, 16 * 512], BF16, offset=81920)
        ident = A("ident", [128, 128], BF16, offset=98304)
        maskt = A("maskt", [128, 128], F32, offset=98560)
        onesb = A("onesb", [128, 1], BF16, offset=99072)
        onescf = A("onescf", [128, 1], F32, offset=99104)
        onesrf = A("onesrf", [1, 128], F32, offset=99136)
        cb = A("cb", [128, 8], F32, offset=99648)
        selb = A("selb", [128, 8], F32, offset=99680)
        thrp = A("thrp", [128, 8], F32, offset=99712)
        thrn = A("thrn", [128, 8], F32, offset=99744)
        gam = A("gam", [1, 8], F32, offset=99776)
        gsum = A("gsum", [1, 8], F32, offset=99808)
        selr = A("selr", [1, 8], F32, offset=99840)
        parts = A("parts", [128, 16], F32, offset=99872)
        stsb = A("stsb", [128, 128], BF16, offset=99936)
        stsb_1 = A("stsb_1", [128, 128], BF16, offset=227712)
        rowA = A("rowA", [1, 512], F32, offset=100224)
        rowB = A("rowB", [1, 512], F32, offset=102272)
        rowC = A("rowC", [1, 512], F32, offset=104320)
        ln_tmp = A("ln_tmp", [128, 512], F32, offset=106368)
        state_bf = A("state_bf", [128, 256], BF16, offset=108416)
        ltot = A("ltot", [128, D], F16, offset=108928)
        hbuf = A("hbuf", [128, 22 * 512], BF16, offset=113024)
        w12 = A("w12", [128, 2 * 2 * 16 * 512], BF16, offset=135552)
        w3s = A("w3s", [128, 2 * 22 * 256], BF16, offset=201088)
        xn2 = A("xn2", [128, 16 * 1024], BF16, offset=113024)
        qf_a = A("qf_a", [128, 8 * 1024], BF16, offset=145792)
        qf_b = A("qf_b", [128, 8 * 1024], BF16, offset=81920)
        ao = A("ao", [128, 16 * 1024], BF16, offset=162176)
        kv = A("kv", [128, 2 * 3 * 1024], BF16, offset=194944)
        wsl = A("wsl", [128, 2 * 16 * 256], BF16, offset=207232)
        ltg_t = A("ltg_t", [128, 2 * 2048], F16, offset=194944)
        accI = A("accI", [128, 2048], F32, offset=203136)
        init_bf = A("init_bf", [128, 2048], BF16, offset=211328)
        wosl = A("wosl", [128, 2 * 16 * 256], BF16, offset=113024)
        pw = A("pw", [128, 2 * 5632], F32, offset=113024)
        qt1 = A("qt1", [128, 5632], F32, offset=158080)
        qt2 = A("qt2", [128, 5632], F32, offset=180608)
        qbt = A("qbt", [128, 2 * 5632], BF16, offset=203136)
        sigt = A("sigt", [128, 2 * 512], BF16, offset=223616)
        x2b = A("x2b", [128, 2 * 512], BF16, offset=225664)

        PS = [es.enter_context(nc.psum_tensor(f"P{i}", [128, 512], F32)) for i in range(8)]

        nc.const_aps.register(F32, 0.0, cb[:, 3:4])
        nc.const_aps.register(F32, 1e-5, cb[:, 0:1])
        nc.const_aps.register(F32, 5e-9, cb[:, 1:2])
        nc.const_aps.register(F32, -5e-9, cb[:, 2:3])

        stsb2 = [stsb, stsb_1]

        def qt(h, c0, c1):
            buf = qf_a if h < 8 else qf_b
            return buf[:, (h % 8) * 1024 + c0:(h % 8) * 1024 + c1]

        p = Prog(nc, es)

        def emit_ln(half, dest):
            t_entry = p.val("t")
            p.wait("scalar", "t", t_entry)
            p.wait("scalar", "v", p.val("v"))
            p.wait("tensor", "s", p.val("s"))
            p.wait("vector", "v", p.val("v"))
            for dc in range(DCN):
                p.emit("vector", lambda e, o=dest(dc),
                       i=stream[:, dc * 1024 + half * 512: dc * 1024 + half * 512 + 512]:
                       e.tensor_copy(o, i), inc="v")
            v_xb = p.val("v")
            p.wait("tensor", "v", v_xb)
            for dc in range(DCN):
                MM(p, PS[0][0:1, :], onesb[:, :], dest(dc), dc == 0, dc == DCN - 1)
            tv = []
            for dc in range(DCN):
                b = dc % 2
                if dc >= 2:
                    p.wait("scalar", "t", tv[dc - 2])
                p.emit("scalar", lambda e, o=x2b[:, b * 512:(b + 1) * 512],
                       i=stream[:, dc * 1024 + half * 512: dc * 1024 + half * 512 + 512]:
                       e.square(o, i), inc="s")
                p.wait("tensor", "s", p.val("s"))
                MM(p, PS[1][0:1, :], onesb[:, :], x2b[:, b * 512:(b + 1) * 512],
                   dc == 0, dc == DCN - 1, inc="t")
                tv.append(p.val("t"))
            v_mm = p.val("t")
            p.wait("scalar", "t", v_mm)
            p.emit("scalar", lambda e: e.activation(rowA[:, :], PS[0][0:1, :], AF.Copy,
                                                    scale=INV_D), inc="s")
            p.wait("scalar", "s", p.val("s"))
            p.emit("scalar", lambda e: e.square(rowB[:, :], rowA[:, :]), inc="s")
            p.wait("vector", "s", p.val("s"))
            p.emit("vector", lambda e: e.scalar_tensor_tensor(rowC[:, :], PS[1][0:1, :],
                                                              INV_D, rowB[:, :],
                                                              ALU.mult, ALU.subtract), inc="v")
            p.wait("scalar", "v", p.val("v"))
            p.emit("scalar", lambda e: e.activation(rowB[:, :], rowC[:, :], AF.Sqrt,
                                                    bias=cb[0:1, 0:1], scale=1.0), inc="s")
            s_end = p.val("s")
            p.wait("vector", "s", s_end)
            p.emit("vector", lambda e: e.reciprocal(rowC[:, :], rowB[:, :]), inc="v")
            p.wait("tensor", "v", p.val("v"))
            MM(p, PS[2][:, :], onesrf[:, :], rowA[:, :], True, True)
            MM(p, PS[3][:, :], onesrf[:, :], rowC[:, :], True, True, inc="t")
            t_end = p.val("t")
            p.wait("vector", "t", t_end)
            for dc in range(DCN):
                p.emit("vector", lambda e, o=ln_tmp[:, :],
                       i0=stream[:, dc * 1024 + half * 512: dc * 1024 + half * 512 + 512],
                       i1=PS[2][:, :]: e.tensor_tensor(o, i0, i1, ALU.subtract), inc="v")
                p.wait("vector", "v", p.val("v"))
                p.emit("vector", lambda e, o=dest(dc), i0=ln_tmp[:, :], i1=PS[3][:, :]:
                       e.tensor_tensor(o, i0, i1, ALU.mult), inc="v")
                p.wait("vector", "v", p.val("v"))
            return p.val("v"), t_end, s_end

        for o, i in ((ident, ident_i), (maskt, maskt_i), (onesb, onesb_i),
                     (onescf, onescf_i), (onesrf, onesrf_i), (cb, cb_i), (selr, selv_i)):
            DMA(p, "gpsimd", "dc0", o[:, :], i[:, :])
        v_consts = p.val("dc0")
        for dc in range(DCN):
            DMA(p, "gpsimd", "dx", stream[:, dc * 1024:(dc + 1) * 1024],
                xt[dc * 128:(dc + 1) * 128, :])
        v_xload = p.val("dx")

        chunks = []
        for r0 in (0, 128):
            chunks.append((0, w1t_sh, r0, 128, HID))
        for r0 in (0, 128):
            chunks.append((1, w2t_sh, r0, 128, HID))
        for r0 in range(0, 704, 128):
            chunks.append((2, w3t_sh, r0, min(128, 704 - r0), D))

        p.emit("vector", lambda e: e.memset(parts[64:128, 9:10], 0.0), inc="v")
        p.emit("vector", lambda e: e.memset(parts[:, 10:16], 0.0), inc="v")
        p.emit("vector", lambda e: e.memset(gsum[0:1, 3:8], 0.0), inc="v")
        red_done = []
        for slot, (mi, src, r0, nr, cw) in enumerate(chunks):
            b = slot % 2
            lane = "dpa" if b == 0 else "dpb"
            if slot >= 2:
                p.wait("sync", "v", red_done[slot - 2])
            v_ld = DMA(p, "sync", lane, pw[0:nr, b * 5632:b * 5632 + cw], src[r0:r0 + nr, :])
            p.wait("vector", lane, v_ld)
            p.emit("vector",
                   lambda e, o=parts[0:nr, slot:slot + 1], i=pw[0:nr, b * 5632:b * 5632 + cw]:
                   e.reduce_sum(o, i, AX.X, apply_absolute_value=True), inc="v")
            red_done.append(p.val("v"))
        p.wait("tensor", "dc0", v_consts)
        p.wait("tensor", "v", p.val("v"))
        MM(p, PS[0][0:1, 0:16], onescf[:, :], parts[:, :], True, True, inc="t")
        p.wait("vector", "t", p.val("t"))
        p.emit("vector", lambda e: e.reduce_sum(gsum[0:1, 0:1], PS[0][0:1, 0:2], AX.X), inc="v")
        p.emit("vector", lambda e: e.reduce_sum(gsum[0:1, 1:2], PS[0][0:1, 2:4], AX.X), inc="v")
        p.emit("vector", lambda e: e.reduce_sum(gsum[0:1, 2:3], PS[0][0:1, 4:10], AX.X), inc="v")
        v_gsum = p.val("v")
        p.wait("gpsimd", "v", v_gsum)
        v_gs = DMA(p, "gpsimd", "dgm", gsrc[:, :], gsum[:, :])
        p.wait("gpsimd", "dgm", v_gs)
        p.emit("gpsimd", lambda e: e.collective_compute(
            "AllReduce", ALU.add, replica_groups=[list(range(NCORES))],
            ins=[gsrc.ap().opt()], outs=[gdst.ap().opt()]), inc="car")
        p.wait("gpsimd", "car", 1)
        v_gam = DMA(p, "gpsimd", "dgm", gam[:, :], gdst[:, :])
        p.wait("tensor", "dx", v_xload)
        p.wait("tensor", "dc0", v_consts)
        p.wait("vector", "dx", v_xload)
        p.wait("vector", "dc0", v_consts)
        p.wait("scalar", "dx", v_xload)
        xdest0 = lambda dc: xn[:, dc * 512:(dc + 1) * 512]
        ln1_pre = emit_ln(0, xdest0)

        p.wait("scalar", "dgm", v_gam)
        p.wait("scalar", "dc0", v_consts)
        p.wait("scalar", "t", ln1_pre[1])
        p.emit("scalar", lambda e: e.activation(rowA[0:1, 0:8], gam[:, :], AF.Copy,
                                                bias=5e-9, scale=0.5 / NW), inc="s")
        p.emit("scalar", lambda e: e.activation(rowB[0:1, 0:8], gam[:, :], AF.Copy,
                                                bias=-5e-9, scale=-0.5 / NW), inc="s")
        p.wait("tensor", "s", p.val("s"))
        p.wait("tensor", "v", ln1_pre[0])
        MM(p, PS[1][:, 0:8], onesrf[:, :], rowA[0:1, 0:8], True, True)
        MM(p, PS[2][:, 0:8], onesrf[:, :], rowB[0:1, 0:8], True, True)
        MM(p, PS[3][:, 0:8], onesrf[:, :], selr[:, :], True, True, inc="t")
        p.wait("scalar", "t", p.val("t"))
        p.emit("scalar", lambda e: e.copy(thrp[:, :], PS[1][:, 0:8]), inc="s")
        p.emit("scalar", lambda e: e.copy(thrn[:, :], PS[2][:, 0:8]), inc="s")
        p.emit("scalar", lambda e: e.copy(selb[:, :], PS[3][:, 0:8]), inc="s")
        s_thr = p.val("s")

        qlocs = {0: q1loc, 1: q2loc, 2: q3loc}
        slot = 0
        store_done = []
        use_done = []
        for (mi, src, r0, nr, cw) in chunks:
            b = slot % 2
            lane = "dpa" if b == 0 else "dpb"
            slane = "dsa" if b == 0 else "dsb"
            if slot >= 2:
                p.wait("sync", "v", use_done[slot - 2])
            else:
                p.wait("sync", "v", red_done[slot - 2])
            v_ld = DMA(p, "sync", lane, pw[0:nr, b * 5632:b * 5632 + cw], src[r0:r0 + nr, :])
            p.wait("vector", lane, v_ld)
            if slot == 0:
                p.wait("vector", "s", s_thr)
            p.emit("vector", lambda e, o=qt1[0:nr, 0:cw], i=pw[0:nr, b * 5632:b * 5632 + cw],
                   s1=thrp[0:nr, mi:mi + 1]: e.tensor_scalar(o, i, s1, None, ALU.is_ge), inc="v")
            p.emit("vector", lambda e, o=qt2[0:nr, 0:cw], i=pw[0:nr, b * 5632:b * 5632 + cw],
                   s1=thrn[0:nr, mi:mi + 1]: e.tensor_scalar(o, i, s1, None, ALU.is_ge), inc="v")
            use_done.append(p.val("v"))
            p.wait("vector", "v", p.val("v"))
            if slot >= 2:
                p.wait("vector", store_done[slot - 2][0], store_done[slot - 2][1])
            p.emit("vector", lambda e, o=qbt[0:nr, b * 5632:b * 5632 + cw],
                   i0=qt1[0:nr, 0:cw], i1=qt2[0:nr, 0:cw]:
                   e.scalar_tensor_tensor(o, i0, -1.0, i1, ALU.add, ALU.add), inc="v")
            p.wait("gpsimd", "v", p.val("v"))
            store_done.append((slane, DMA(p, "gpsimd", slane, qlocs[mi][r0:r0 + nr, :],
                                          qbt[0:nr, b * 5632:b * 5632 + cw])))
            slot += 1
            if slot in (2, 4, 10):
                csem, locb, outb = {2: ("cq1", q1loc, q1t), 4: ("cq2", q2loc, q2t),
                                    10: ("cq3", q3loc, q3t)}[slot]
                p.wait("gpsimd", store_done[-1][0], store_done[-1][1])
                p.wait("gpsimd", store_done[-2][0], store_done[-2][1])
                p.emit("gpsimd", lambda e, i=locb, o=outb: e.collective_compute(
                    "AllGather", ALU.bypass, replica_groups=[list(range(NCORES))],
                    ins=[i.ap().opt()], outs=[o.ap().opt()]), inc=csem)
        v_prolog_ve = p.val("v")
        prolog_stores = [(la, p.val(la)) for la in ("dsa", "dsb")]
        for ci, (csrc, loc, csem, outb) in enumerate((
                (wqt_sh, wqloc, "cwq", wqt), (wkt_sh, wkloc, "cwk", wkt),
                (wvt_sh, wvloc, "cwv", wvt), (wot_sh, woloc, "cwo", wot))):
            lane = "dla" if ci % 2 == 0 else "dlb"
            v_cast = DMA(p, "gpsimd", lane, loc[:, :], csrc[:, :])
            p.wait("gpsimd", lane, v_cast)
            p.emit("gpsimd", lambda e, i=loc, o=outb: e.collective_compute(
                "AllGather", ALU.bypass, replica_groups=[list(range(NCORES))],
                ins=[i.ap().opt()], outs=[o.ap().opt()]), inc=csem)

        fs = {}

        ffn_no = [0]

        def emit_ffn(halves=(0, 1), hpasses=(0, 1), do_p2=True):
            ffn_no[0] += 1
            first_call = ffn_no[0] == 1
            xdest = lambda dc: xn[:, dc * 512:(dc + 1) * 512]
            for half in halves:
                if first_call and half == 0:
                    v_xn, _, s_ln = ln1_pre
                else:
                    v_xn, _, s_ln = emit_ln(half, xdest)
                p.wait("tensor", "v", v_xn)
                p.wait("tensor", "s", s_ln)
                for hpass in hpasses:
                    hids = list(range(hpass * 22, hpass * 22 + 22))
                    groups = [hids[i:i + 4] for i in range(0, 22, 4)]
                    for gi, grp in enumerate(groups):
                        buf = gi % 2
                        if first_call and half == 0 and hpass == 0 and gi == 0:
                            p.wait("sync", "cq1", 1)
                            p.wait("sync", "cq2", 1)
                        wlane = "dwa" if buf == 0 else "dwb"
                        if first_call and half == 0 and hpass == 0 and gi in (0, 1):
                            p.wait("sync", "v", v_prolog_ve)
                            for la, lv in prolog_stores:
                                p.wait("sync", la, lv)
                        p.wait("sync", "t", fs.get(("w12f", buf), 0))
                        gw = len(grp) * 128
                        for mat, qsrc in ((0, q1t), (1, q2t)):
                            base = (buf * 2 + mat) * 16 * 512
                            dst = w12[:, base:base + 16 * 512].rearrange(
                                "p (dc c) -> p dc c", dc=16)[:, :, 0:gw]
                            srcap = qsrc.rearrange("(dc p) h -> p dc h", p=128)[
                                :, :, grp[0] * 128:grp[0] * 128 + gw]
                            DMA(p, "sync", wlane, dst, srcap)
                        v_slab = p.val(wlane)
                        p.wait("tensor", wlane, v_slab)
                        for li, hc in enumerate(grp):
                            pb = (li % 2) * 2
                            p.wait("tensor", "s", fs.get(("p1s", pb), 0))
                            p.wait("tensor", "v", fs.get(("p1v", pb), 0))
                            for dc in range(DCN):
                                base = ((buf * 2 + 0) * 16 + dc) * 512 + li * 128
                                MM(p, PS[pb][:, :], w12[:, base:base + 128],
                                   xn[:, dc * 512:(dc + 1) * 512], dc == 0, dc == DCN - 1,
                                   inc=("t" if dc == DCN - 1 else None))
                            v_x1 = p.val("t")
                            for dc in range(DCN):
                                base = ((buf * 2 + 1) * 16 + dc) * 512 + li * 128
                                MM(p, PS[pb + 1][:, :], w12[:, base:base + 128],
                                   xn[:, dc * 512:(dc + 1) * 512], dc == 0, dc == DCN - 1,
                                   inc=("t" if dc == DCN - 1 else None))
                            v_x2 = p.val("t")
                            sb = li % 2
                            p.wait("scalar", "t", v_x1)
                            p.wait("scalar", "v", fs.get(("sig", sb), 0))
                            p.emit("scalar", lambda e, o=sigt[:, sb * 512:(sb + 1) * 512],
                                   i=PS[pb][:, :]: e.activation(o, i, AF.Sigmoid), inc="s")
                            fs[("p1s", pb)] = p.val("s")
                            hx = hc - hpass * 22
                            p.wait("vector", "t", v_x2)
                            p.wait("vector", "s", p.val("s"))
                            p.emit("vector", lambda e, o=hbuf[:, hx * 512:(hx + 1) * 512],
                                   i0=sigt[:, sb * 512:(sb + 1) * 512], i1=PS[pb + 1][:, :]:
                                   e.tensor_tensor(o, i0, i1, ALU.mult), inc="v")
                            fs[("p1v", pb)] = p.val("v")
                            fs[("sig", sb)] = p.val("v")
                        fs[("w12f", buf)] = p.val("t")
                    v_hready = p.val("v")
                    for dg in (range(8) if do_p2 else range(0)):
                        buf = dg % 2
                        if first_call and half == 0 and hpass == 0 and dg == 0:
                            p.wait("sync", "cq3", 1)
                        wlane = "d3a" if buf == 0 else "d3b"
                        if first_call and half == 0 and hpass == 0 and dg in (0, 1):
                            p.wait("sync", "v", v_prolog_ve)
                            for la, lv in prolog_stores:
                                p.wait("sync", la, lv)
                        p.wait("sync", "t", fs.get(("w3f", buf), 0))
                        dst = w3s[:, buf * 22 * 256:(buf + 1) * 22 * 256].rearrange(
                            "p (i c) -> p i c", i=22)
                        srcap = q3t.rearrange("(i p) d -> p i d", p=128)[
                            :, hids[0]:hids[0] + 22, dg * 256:(dg + 1) * 256]
                        DMA(p, "sync", wlane, dst, srcap)
                        v_slab = p.val(wlane)
                        p.wait("tensor", wlane, v_slab)
                        p.wait("tensor", "v", v_hready)
                        for dc2 in range(2):
                            ob = 4 + dc2
                            p.wait("tensor", "v", fs.get(("p2v", ob), 0))
                            for i in range(22):
                                base = (buf * 22 + i) * 256 + dc2 * 128
                                MM(p, PS[ob][:, :], w3s[:, base:base + 128],
                                   hbuf[:, i * 512:(i + 1) * 512], i == 0, i == 21,
                                   inc=("t" if i == 21 else None))
                            v_out = p.val("t")
                            dci = dg * 2 + dc2
                            p.wait("vector", "t", v_out)
                            p.emit("vector", lambda e,
                                   o=stream[:, dci * 1024 + half * 512:dci * 1024 + half * 512 + 512],
                                   i0=PS[ob][:, :]:
                                   e.scalar_tensor_tensor(o, i0, 0.5, o, ALU.mult, ALU.add),
                                   inc="v")
                            fs[("p2v", ob)] = p.val("v")
                        fs[("w3f", buf)] = p.val("t")

        def emit_attn():
            for half in range(2):
                dest = lambda dc, h=half: xn2[:, dc * 1024 + h * 512: dc * 1024 + h * 512 + 512]
                v_xn2, _, s_ln = emit_ln(half, dest)
            p.wait("tensor", "v", v_xn2)
            p.wait("tensor", "s", s_ln)

            t_pre_attn = p.val("t")
            for g in range(8):
                h0 = g * 2
                bufA, bufB = 0, 1
                if g == 0:
                    p.wait("sync", "cwq", 1)
                    p.wait("sync", "cwk", 1)
                    p.wait("sync", "cwv", 1)
                    p.wait("sync", "t", t_pre_attn)
                p.wait("sync", "t", fs.get(("wslf", bufA), 0))
                DMA(p, "sync", "dqa",
                    wsl[:, bufA * 16 * 256:(bufA + 1) * 16 * 256].rearrange(
                        "p (dc c) -> p dc c", dc=16),
                    wqt.rearrange("(dc p) d -> p dc d", p=128)[
                        :, :, h0 * 128:h0 * 128 + 256])
                v_wq = ("dqa", p.val("dqa"))
                p.wait("sync", "t", fs.get(("wslf", bufB), 0))
                DMA(p, "sync", "dqb",
                    wsl[:, bufB * 16 * 256:(bufB + 1) * 16 * 256].rearrange(
                        "p (dc c) -> p dc c", dc=16),
                    wkt.rearrange("(dc p) d -> p dc d", p=128)[
                        :, :, h0 * 128:h0 * 128 + 256])
                v_wk = ("dqb", p.val("dqb"))

                def proj(bufX, v_w, evac):
                    p.wait("tensor", v_w[0], v_w[1])
                    for hh in range(2):
                        for colh in range(2):
                            pb = colh
                            p.wait("tensor", "s", fs.get(("pA", pb), 0))
                            p.wait("tensor", "v", fs.get(("pT", pb), 0))
                            for dc in range(DCN):
                                base = (bufX * 16 + dc) * 256 + hh * 128
                                MM(p, PS[pb][:, :], wsl[:, base:base + 128],
                                   xn2[:, dc * 1024 + colh * 512:dc * 1024 + colh * 512 + 512],
                                   dc == 0, dc == DCN - 1, inc=("t" if dc == DCN - 1 else None))
                            p.wait("scalar", "t", p.val("t"))
                            evac(h0 + hh, hh, colh, PS[pb][:, :])
                            fs[("pA", pb)] = p.val("s")

                def transposes(slot, src_off, dst_off):
                    p.wait("tensor", "s", p.val("s"))
                    for h4 in range(2):
                        pb = h4
                        p.wait("tensor", "v", fs.get(("pT", pb), 0))
                        p.wait("tensor", "s", fs.get(("pA", pb), 0))
                        for t4 in range(4):
                            tb = h4 * 4 + t4
                            MM(p, PS[pb][:, t4 * 128:t4 * 128 + 128],
                               kv[:, src_off + tb * 128:src_off + tb * 128 + 128],
                               ident[:, :], True, True, inc=("t" if t4 == 3 else None))
                        v_tr = p.val("t")
                        p.wait("vector", "t", v_tr)
                        p.emit("vector", lambda e, o=kv[:, dst_off + h4 * 512:dst_off + h4 * 512 + 512],
                               i=PS[pb][:, :]: e.tensor_copy(o, i), inc="v")
                        fs[("pT", pb)] = p.val("v")

                proj(bufA, v_wq, lambda h, hh, colh, ps: (
                    p.emit("scalar", lambda e, o=qt(h, colh * 512, colh * 512 + 512), i=ps:
                           e.activation(o, i, AF.Copy, scale=QSCALE), inc="s")))
                fs[("wslf", bufA)] = p.val("t")
                p.wait("sync", "t", fs[("wslf", bufA)])
                DMA(p, "sync", "dqa",
                    wsl[:, bufA * 16 * 256:(bufA + 1) * 16 * 256].rearrange(
                        "p (dc c) -> p dc c", dc=16),
                    wvt.rearrange("(dc p) d -> p dc d", p=128)[
                        :, :, h0 * 128:h0 * 128 + 256])
                v_wv = ("dqa", p.val("dqa"))

                proj(bufB, v_wk, lambda h, hh, colh, ps: (
                    p.emit("scalar", lambda e, o=kv[:, (hh * 3 + 0) * 1024 + colh * 512:
                           (hh * 3 + 0) * 1024 + colh * 512 + 512], i=ps:
                           e.sign(o, i), inc="s")))
                for hh in range(2):
                    transposes(hh, (hh * 3 + 0) * 1024, (hh * 3 + 1) * 1024)
                fs[("wslf", bufB)] = p.val("t")

                proj(bufA, v_wv, lambda h, hh, colh, ps: (
                    p.emit("scalar", lambda e, o=kv[:, (hh * 3 + 2) * 1024 + colh * 512:
                           (hh * 3 + 2) * 1024 + colh * 512 + 512], i=ps:
                           e.sign(o, i), inc="s")))
                for hh in range(2):
                    transposes(hh, (hh * 3 + 2) * 1024, (hh * 3 + 2) * 1024)
                fs[("wslf", bufA)] = p.val("t")
                p.wait("tensor", "v", p.val("v"))
                p.wait("tensor", "s", p.val("s"))

                NB = 8
                for n in range(NB):
                    for hh in range(2):
                        h = h0 + hh
                        sT_b = PS[2 + hh]
                        out_b = PS[0 + hh]
                        sbank = 4 + (n % 2) * 2 + hh
                        kfm = kv[:, (hh * 3 + 0) * 1024 + n * 128:(hh * 3 + 0) * 1024 + n * 128 + 128]
                        kT = kv[:, (hh * 3 + 1) * 1024 + n * 128:(hh * 3 + 1) * 1024 + n * 128 + 128]
                        vT = kv[:, (hh * 3 + 2) * 1024 + n * 128:(hh * 3 + 2) * 1024 + n * 128 + 128]
                        qc = qt(h, n * 128, n * 128 + 128)
                        p.wait("tensor", "v", fs.get(("sT_free", hh), 0))
                        MM(p, sT_b[:, 0:128], kfm, qc, True, True, inc="t")
                        v_st = p.val("t")
                        p.wait("vector", "t", v_st)
                        p.wait("vector", "t", fs.get(("stsb_free", hh), 0))
                        p.emit("vector", lambda e, o=stsb2[hh][:, :], i0=sT_b[:, 0:128],
                               i1=maskt[:, :]: e.tensor_tensor(o, i0, i1, ALU.mult), inc="v")
                        fs[("sT_free", hh)] = p.val("v")
                        v_mask = p.val("v")
                        p.wait("tensor", "v", v_mask)
                        p.wait("tensor", "v", fs.get(("out_free", hh), 0))
                        p.wait("tensor", "v", fs.get(("sbf", hh), 0))
                        if n == 0:
                            MM(p, out_b[:, 0:128], vT, stsb2[hh][:, :], True, True, inc="t")
                        else:
                            MM(p, out_b[:, 0:128], vT, stsb2[hh][:, :], True, False)
                            MM(p, out_b[:, 0:128], state_bf[:, hh * 128:hh * 128 + 128],
                               qc, False, True, inc="t")
                        v_out = p.val("t")
                        fs[("stsb_free", hh)] = v_out
                        p.wait("vector", "t", v_out)
                        p.emit("vector", lambda e, o=ao[:, h * 1024 + n * 128:h * 1024 + n * 128 + 128],
                               i=out_b[:, 0:128]: e.tensor_copy(o, i), inc="v")
                        fs[("out_free", hh)] = p.val("v")
                        p.wait("tensor", "v", fs.get(("stev", sbank), 0))
                        MM(p, PS[sbank][:, 0:128], kT, vT, True, True, inc="t")
                        v_ckv = p.val("t")
                        p.wait("vector", "t", v_ckv)
                        stf = ln_tmp[:, hh * 128:hh * 128 + 128]
                        if n == 0:
                            p.emit("vector", lambda e, o=stf,
                                   i=PS[sbank][:, 0:128]: e.tensor_copy(o, i), inc="v")
                        else:
                            p.emit("vector", lambda e, o=stf, i0=stf,
                                   i1=PS[sbank][:, 0:128]:
                                   e.tensor_tensor(o, i0, i1, ALU.add), inc="v")
                        fs[("stev", sbank)] = p.val("v")
                        if n < NB - 1:
                            p.emit("vector", lambda e, o=state_bf[:, hh * 128:hh * 128 + 128],
                                   i=stf: e.tensor_copy(o, i), inc="v")
                        else:
                            p.emit("vector", lambda e, o=ltot[:, h * 128:h * 128 + 128],
                                   i=stf: e.tensor_copy(o, i), inc="v")
                        fs[("sbf", hh)] = p.val("v")
                fs[("pT", 0)] = max(fs[("pT", 0)], fs[("out_free", 0)])
                fs[("pT", 1)] = max(fs[("pT", 1)], fs[("out_free", 1)])
                if g == 3:
                    p.wait("gpsimd", "v", p.val("v"))
                    v_lo = DMA(p, "gpsimd", "dgm", ltoc_lo[:, :], ltot[:, 0:1024])
                    p.wait("gpsimd", "dgm", v_lo)
                    p.emit("gpsimd", lambda e: e.collective_compute(
                        "AllGather", ALU.bypass, replica_groups=[list(range(NCORES))],
                        ins=[ltoc_lo.ap().opt()], outs=[ltg_lo.ap().opt()]), inc="cltg")

            v_ltot = p.val("v")
            p.wait("gpsimd", "v", v_ltot)
            v_hi = DMA(p, "gpsimd", "dgm", ltoc_hi[:, :], ltot[:, 1024:2048])
            p.wait("gpsimd", "dgm", v_hi)
            p.emit("gpsimd", lambda e: e.collective_compute(
                "AllGather", ALU.bypass, replica_groups=[list(range(NCORES))],
                ins=[ltoc_hi.ap().opt()], outs=[ltg_hi.ap().opt()]), inc="cltg2")
            p.emit("vector", lambda e: e.memset(accI[:, :], 0.0), inc="v")
            p.wait("vector", "v", p.val("v"))
            p.wait("gpsimd", "cltg", 1)
            p.wait("gpsimd", "cltg2", 1)
            for part, ltgb in ((0, ltg_lo), (1, ltg_hi)):
                for j in range(NCORES):
                    b = j % 2
                    llane = "dla" if b == 0 else "dlb"
                    p.wait("gpsimd", "v", fs.get(("ltg", b), 0))
                    v_ld = DMA(p, "gpsimd", llane, ltg_t[:, b * 2048:b * 2048 + 1024],
                               ltgb[j * 128:(j + 1) * 128, :])
                    p.wait("vector", llane, v_ld)
                    p.emit("vector", lambda e, i0=ltg_t[:, b * 2048:b * 2048 + 1024],
                           s1=selb[:, j:j + 1],
                           o=accI[:, part * 1024:(part + 1) * 1024]:
                           e.scalar_tensor_tensor(o, i0, s1, o, ALU.mult, ALU.add), inc="v")
                    fs[("ltg", b)] = p.val("v")
                    p.wait("vector", "v", p.val("v"))
            p.emit("vector", lambda e: e.tensor_copy(init_bf[:, :], accI[:, :]), inc="v")
            v_init = p.val("v")
            p.wait("tensor", "v", v_init)
            for h in range(NH):
                for colh in range(2):
                    pb = colh
                    p.wait("tensor", "v", fs.get(("pI", pb), 0))
                    MM(p, PS[pb][:, :], init_bf[:, h * 128:(h + 1) * 128],
                       qt(h, colh * 512, colh * 512 + 512),
                       True, True, inc="t")
                    v_mmi = p.val("t")
                    p.wait("vector", "t", v_mmi)
                    p.emit("vector", lambda e, o=ao[:, h * 1024 + colh * 512:
                           h * 1024 + colh * 512 + 512], i0=PS[pb][:, :]:
                           e.tensor_tensor(o, o, i0, ALU.add), inc="v")
                    fs[("pI", pb)] = p.val("v")
            v_ao = p.val("v")
            p.wait("tensor", "v", v_ao)
            for half in range(2):
                for dg in range(8):
                    buf = dg % 2
                    if half == 0 and dg == 0:
                        p.wait("sync", "cwo", 1)
                        p.wait("sync", "v", v_ao)
                    wlane = "dqa" if buf == 0 else "dqb"
                    p.wait("sync", "t", fs.get(("wof", buf), 0))
                    DMA(p, "sync", wlane,
                        wosl[:, buf * 16 * 256:(buf + 1) * 16 * 256].rearrange(
                            "p (dc c) -> p dc c", dc=16),
                        wot.rearrange("(dc p) d -> p dc d", p=128)[
                            :, :, dg * 256:(dg + 1) * 256])
                    v_slab = p.val(wlane)
                    p.wait("tensor", wlane, v_slab)
                    for dc2 in range(2):
                        pb = dc2
                        p.wait("tensor", "v", fs.get(("pI", pb), 0))
                        for dc in range(DCN):
                            base = (buf * 16 + dc) * 256 + dc2 * 128
                            MM(p, PS[pb][:, :], wosl[:, base:base + 128],
                               ao[:, dc * 1024 + half * 512:dc * 1024 + half * 512 + 512],
                               dc == 0, dc == DCN - 1, inc=("t" if dc == DCN - 1 else None))
                        v_mmo = p.val("t")
                        dci = dg * 2 + dc2
                        p.wait("vector", "t", v_mmo)
                        p.emit("vector", lambda e,
                               o=stream[:, dci * 1024 + half * 512:dci * 1024 + half * 512 + 512],
                               i0=PS[pb][:, :]: e.tensor_tensor(o, o, i0, ALU.add), inc="v")
                        fs[("pI", pb)] = p.val("v")
                    fs[("wof", buf)] = p.val("t")

        if upto == "ln1":
            emit_ln(0, lambda dc: xn[:, dc * 512:(dc + 1) * 512])
        elif upto == "p1":
            emit_ffn(halves=(0,), hpasses=(0,), do_p2=False)
        elif upto == "p2":
            emit_ffn(halves=(0,), hpasses=(0,), do_p2=True)
        elif upto != "prolog":
            emit_ffn()
        if upto in ("attn", "all"):
            emit_attn()
        if upto == "all":
            p.wait("sync", "v", p.val("v"))
            for _pb in (0, 2):
                fs[("p1s", _pb)] = p.val("s")
                fs[("p1v", _pb)] = p.val("v")
            for _ob in (4, 5):
                fs[("p2v", _ob)] = p.val("v")
            emit_ffn()

        v_last = p.val("v")
        p.wait("gpsimd", "v", v_last)
        for dc in range(DCN):
            DMA(p, "gpsimd", "st", yt[dc * 128:(dc + 1) * 128, :],
                stream[:, dc * 1024:(dc + 1) * 1024])
        p.wait("gpsimd", "st", p.val("st"))

        block = es.enter_context(nc.Block())

        @block.sync
        def _(eng):
            for f in p.ops["sync"]:
                f(eng)

        @block.tensor
        def _(eng):
            for f in p.ops["tensor"]:
                f(eng)

        @block.vector
        def _(eng):
            for f in p.ops["vector"]:
                f(eng)

        @block.scalar
        def _(eng):
            for f in p.ops["scalar"]:
                f(eng)

        @block.gpsimd
        def _(eng):
            for f in p.ops["gpsimd"]:
                f(eng)

    return nc


def make_inputs(L0_attn_x, L0_attn_Wq, L0_attn_Wk, L0_attn_Wv, L0_attn_Wo,
                L0_ffn_w1, L0_ffn_w2, L0_ffn_w3):
    xf = np.asarray(L0_attn_x, np.float32).reshape(NCORES * TOK, D)
    w1t = np.ascontiguousarray(np.asarray(L0_ffn_w1, np.float32).T)
    w2t = np.ascontiguousarray(np.asarray(L0_ffn_w2, np.float32).T)
    w3t = np.ascontiguousarray(np.asarray(L0_ffn_w3, np.float32).T)
    wqt = np.ascontiguousarray(np.asarray(L0_attn_Wq, np.float32).T)
    wkt = np.ascontiguousarray(np.asarray(L0_attn_Wk, np.float32).T)
    wvt = np.ascontiguousarray(np.asarray(L0_attn_Wv, np.float32).T)
    wot = np.ascontiguousarray(np.asarray(L0_attn_Wo, np.float32).T)
    ident = np.eye(128, dtype=BFNP)
    maskt = np.triu(np.ones((128, 128), np.float32))
    onesb = np.ones((128, 1), BFNP)
    onescf = np.ones((128, 1), np.float32)
    onesrf = np.ones((1, 128), np.float32)
    cbv = np.broadcast_to(np.array([[1e-5, 5e-9, -5e-9, 0, 0, 0, 0, 0]], np.float32), (128, 8)).copy()
    ims = []
    for r in range(NCORES):
        sel = np.zeros((1, 8), np.float32)
        for j in range((r // 4) * 4, r):
            sel[0, j] = 1.0
        ims.append({
            "xt": np.ascontiguousarray(xf[r * TOK:(r + 1) * TOK, :].T),
            "w1t_sh": np.ascontiguousarray(w1t[r * 256:(r + 1) * 256, :]),
            "w2t_sh": np.ascontiguousarray(w2t[r * 256:(r + 1) * 256, :]),
            "w3t_sh": np.ascontiguousarray(w3t[r * 704:(r + 1) * 704, :]),
            "wqt_sh": np.ascontiguousarray(wqt[r * 256:(r + 1) * 256, :]),
            "wkt_sh": np.ascontiguousarray(wkt[r * 256:(r + 1) * 256, :]),
            "wvt_sh": np.ascontiguousarray(wvt[r * 256:(r + 1) * 256, :]),
            "wot_sh": np.ascontiguousarray(wot[r * 256:(r + 1) * 256, :]),
            "ident_i": ident, "maskt_i": maskt, "onesb_i": onesb,
            "onescf_i": onescf, "onesrf_i": onesrf, "cb_i": cbv, "selv_i": sel,
        })
    return ims


def run(upto="all", **inputs):
    nc = build(upto)
    ims = make_inputs(**inputs)
    res = run_bass_kernel_spmd(nc, ims, core_ids=list(range(NCORES)))
    outs = [np.asarray(res.results[r]["yt"]).T for r in range(NCORES)]
    return np.concatenate(outs, axis=0).reshape(2, 4096, 2048)


def kernel(**inputs):
    return run("all", **inputs)


# version 06
# speedup vs baseline: 1.0506x; 1.0506x over previous
import contextlib
import numpy as np
import ml_dtypes
import concourse.bass as bass
import concourse.mybir as mybir
from concourse.bass_utils import run_bass_kernel_spmd

F32 = mybir.dt.float32
BF16 = mybir.dt.bfloat16
F16 = mybir.dt.float16
AF = mybir.ActivationFunctionType
ALU = mybir.AluOpType
AX = mybir.AxisListType
BFNP = ml_dtypes.bfloat16

NCORES = 8
D = 2048
HID = 5632
TOK = 1024
DCN = 16
NH = 16
NCH = 16
QSCALE = float(1.0 / np.sqrt(128.0))
NW = float(HID * D)
INV_D = 1.0 / 2048.0

SEMS = ("dc0", "dx", "dgm", "dpa", "dpb", "dsa", "dsb", "dla", "dlb",
        "dwa", "dwb", "d3a", "d3b", "dqa", "dqb",
        "t", "v", "s", "st",
        "car", "cq1", "cq2", "cq3", "cwat", "cltg", "cltg2")


class Prog:
    def __init__(self, nc, es):
        self.nc = nc
        self.ops = {k: [] for k in ("sync", "tensor", "vector", "scalar", "gpsimd")}
        self.sems = {n: es.enter_context(nc.semaphore(n)) for n in SEMS}
        self.cnt = {n: 0 for n in SEMS}

    def emit(self, eng, fn, inc=None, n=1):
        if inc is None:
            self.ops[eng].append(fn)
            return None
        if self.cnt[inc] > 0 and eng not in ("sync", "gpsimd"):
            self.wait(eng, inc, self.cnt[inc])
        self.cnt[inc] += n
        s = self.sems[inc]
        self.ops[eng].append(lambda e, fn=fn, s=s, n=n: fn(e).then_inc(s, n))
        return self.cnt[inc]

    def wait(self, eng, inc, val):
        if not val or val <= 0:
            return
        s = self.sems[inc]
        self.ops[eng].append(lambda e, s=s, v=val: e.wait_ge(s, v))

    def val(self, inc):
        return self.cnt[inc]


def DMA(p, q, sem, out, in_):
    return p.emit(q, lambda e, o=out, i=in_: e.dma_start(out=o, in_=i), inc=sem, n=16)


def MM(p, out, lhsT, rhs, start, stop, inc=None):
    return p.emit("tensor",
                  lambda e, o=out, l=lhsT, r=rhs, a=start, b=stop: e.matmul(o, l, r, start=a, stop=b),
                  inc=inc)


def build(upto="all"):
    nc = bass.Bass("TRN2", target_bir_lowering=False, debug=False)

    xt = nc.dram_tensor("xt", [D, TOK], F32, kind="ExternalInput").ap()
    w1t_sh = nc.dram_tensor("w1t_sh", [256, HID], F32, kind="ExternalInput").ap()
    w2t_sh = nc.dram_tensor("w2t_sh", [256, HID], F32, kind="ExternalInput").ap()
    w3t_sh = nc.dram_tensor("w3t_sh", [704, D], F32, kind="ExternalInput").ap()
    wqt_sh = nc.dram_tensor("wqt_sh", [256, D], F32, kind="ExternalInput").ap()
    wkt_sh = nc.dram_tensor("wkt_sh", [256, D], F32, kind="ExternalInput").ap()
    wvt_sh = nc.dram_tensor("wvt_sh", [256, D], F32, kind="ExternalInput").ap()
    wot_sh = nc.dram_tensor("wot_sh", [256, D], F32, kind="ExternalInput").ap()
    ident_i = nc.dram_tensor("ident_i", [128, 128], BF16, kind="ExternalInput").ap()
    maskt_i = nc.dram_tensor("maskt_i", [128, 128], F32, kind="ExternalInput").ap()
    onesb_i = nc.dram_tensor("onesb_i", [128, 1], BF16, kind="ExternalInput").ap()
    onescf_i = nc.dram_tensor("onescf_i", [128, 1], F32, kind="ExternalInput").ap()
    onesrf_i = nc.dram_tensor("onesrf_i", [1, 128], F32, kind="ExternalInput").ap()
    cb_i = nc.dram_tensor("cb_i", [128, 8], F32, kind="ExternalInput").ap()
    selv_i = nc.dram_tensor("selv_i", [1, 8], F32, kind="ExternalInput").ap()

    yt = nc.dram_tensor("yt", [D, TOK], F32, kind="ExternalOutput").ap()

    gsrc = nc.dram_tensor("gsrc", [1, 8], F32)
    gdst = nc.dram_tensor("gdst", [1, 8], F32)
    q1loc = nc.dram_tensor("q1loc", [256, HID], BF16)
    q2loc = nc.dram_tensor("q2loc", [256, HID], BF16)
    q3loc = nc.dram_tensor("q3loc", [704, D], BF16)
    watloc = nc.dram_tensor("watloc", [256, 4 * D], BF16)
    q1t = nc.dram_tensor("q1t", [D, HID], BF16, addr_space="Shared")
    q2t = nc.dram_tensor("q2t", [D, HID], BF16, addr_space="Shared")
    q3t = nc.dram_tensor("q3t", [HID, D], BF16, addr_space="Shared")
    wat = nc.dram_tensor("wat", [D, 4 * D], BF16, addr_space="Shared")
    ltoc_lo = nc.dram_tensor("ltoc_lo", [128, 1024], F16)
    ltoc_hi = nc.dram_tensor("ltoc_hi", [128, 1024], F16)
    ltg_lo = nc.dram_tensor("ltg_lo", [128 * NCORES, 1024], F16, addr_space="Shared")
    ltg_hi = nc.dram_tensor("ltg_hi", [128 * NCORES, 1024], F16, addr_space="Shared")

    es = contextlib.ExitStack()
    with es:
        A = nc.alloc_sbuf_tensor_at
        stream = A("stream", [128, 16 * 1024], F32, offset=16384)
        xn = A("xn", [128, 16 * 512], BF16, offset=81920)
        ident = A("ident", [128, 128], BF16, offset=98304)
        maskt = A("maskt", [128, 128], F32, offset=98560)
        onesb = A("onesb", [128, 1], BF16, offset=99072)
        onescf = A("onescf", [128, 1], F32, offset=99104)
        onesrf = A("onesrf", [1, 128], F32, offset=99136)
        cb = A("cb", [128, 8], F32, offset=99648)
        selb = A("selb", [128, 8], F32, offset=99680)
        thrp = A("thrp", [128, 8], F32, offset=99712)
        thrn = A("thrn", [128, 8], F32, offset=99744)
        gam = A("gam", [1, 8], F32, offset=99776)
        gsum = A("gsum", [1, 8], F32, offset=99808)
        selr = A("selr", [1, 8], F32, offset=99840)
        parts = A("parts", [128, 16], F32, offset=99872)
        stsb = A("stsb", [128, 128], BF16, offset=99936)
        stsb_1 = A("stsb_1", [128, 128], BF16, offset=227712)
        rowA = A("rowA", [1, 512], F32, offset=100224)
        rowB = A("rowB", [1, 512], F32, offset=102272)
        rowC = A("rowC", [1, 512], F32, offset=104320)
        ln_tmp = A("ln_tmp", [128, 512], F32, offset=106368)
        state_bf = A("state_bf", [128, 256], BF16, offset=108416)
        ltot = A("ltot", [128, D], F16, offset=108928)
        hbuf = A("hbuf", [128, 22 * 512], BF16, offset=113024)
        w12 = A("w12", [128, 2 * 2 * 16 * 512], BF16, offset=135552)
        w3s = A("w3s", [128, 2 * 22 * 256], BF16, offset=201088)
        xn2 = A("xn2", [128, 16 * 1024], BF16, offset=113024)
        qf_a = A("qf_a", [128, 8 * 1024], BF16, offset=145792)
        qf_b = A("qf_b", [128, 8 * 1024], BF16, offset=81920)
        ao = A("ao", [128, 16 * 1024], BF16, offset=162176)
        kv = A("kv", [128, 2 * 3 * 1024], BF16, offset=194944)
        wsl = A("wsl", [128, 2 * 16 * 256], BF16, offset=207232)
        ltg_t = A("ltg_t", [128, 2 * 2048], F16, offset=194944)
        accI = A("accI", [128, 2048], F32, offset=203136)
        init_bf = A("init_bf", [128, 2048], BF16, offset=211328)
        wosl = A("wosl", [128, 2 * 16 * 256], BF16, offset=113024)
        pw = A("pw", [128, 2 * 5632], F32, offset=113024)
        qt1 = A("qt1", [128, 5632], F32, offset=158080)
        qt2 = A("qt2", [128, 5632], F32, offset=180608)
        qbt = A("qbt", [128, 2 * 5632], BF16, offset=203136)
        sigt = A("sigt", [128, 2 * 512], BF16, offset=223616)
        x2b = A("x2b", [128, 2 * 512], BF16, offset=225664)

        PS = [es.enter_context(nc.psum_tensor(f"P{i}", [128, 512], F32)) for i in range(8)]

        nc.const_aps.register(F32, 0.0, cb[:, 3:4])
        nc.const_aps.register(F32, 1e-5, cb[:, 0:1])
        nc.const_aps.register(F32, 5e-9, cb[:, 1:2])
        nc.const_aps.register(F32, -5e-9, cb[:, 2:3])

        stsb2 = [stsb, stsb_1]

        def qt(h, c0, c1):
            buf = qf_a if h < 8 else qf_b
            return buf[:, (h % 8) * 1024 + c0:(h % 8) * 1024 + c1]

        p = Prog(nc, es)

        def emit_ln(half, dest):
            t_entry = p.val("t")
            p.wait("scalar", "t", t_entry)
            p.wait("scalar", "v", p.val("v"))
            p.wait("tensor", "s", p.val("s"))
            p.wait("vector", "v", p.val("v"))
            for dc in range(DCN):
                p.emit("vector", lambda e, o=dest(dc),
                       i=stream[:, dc * 1024 + half * 512: dc * 1024 + half * 512 + 512]:
                       e.tensor_copy(o, i), inc="v")
            v_xb = p.val("v")
            p.wait("tensor", "v", v_xb)
            for dc in range(DCN):
                MM(p, PS[0][0:1, :], onesb[:, :], dest(dc), dc == 0, dc == DCN - 1)
            tv = []
            for dc in range(DCN):
                b = dc % 2
                if dc >= 2:
                    p.wait("scalar", "t", tv[dc - 2])
                p.emit("scalar", lambda e, o=x2b[:, b * 512:(b + 1) * 512],
                       i=stream[:, dc * 1024 + half * 512: dc * 1024 + half * 512 + 512]:
                       e.square(o, i), inc="s")
                p.wait("tensor", "s", p.val("s"))
                MM(p, PS[1][0:1, :], onesb[:, :], x2b[:, b * 512:(b + 1) * 512],
                   dc == 0, dc == DCN - 1, inc="t")
                tv.append(p.val("t"))
            v_mm = p.val("t")
            p.wait("scalar", "t", v_mm)
            p.emit("scalar", lambda e: e.activation(rowA[:, :], PS[0][0:1, :], AF.Copy,
                                                    scale=INV_D), inc="s")
            p.wait("scalar", "s", p.val("s"))
            p.emit("scalar", lambda e: e.square(rowB[:, :], rowA[:, :]), inc="s")
            p.wait("vector", "s", p.val("s"))
            p.emit("vector", lambda e: e.scalar_tensor_tensor(rowC[:, :], PS[1][0:1, :],
                                                              INV_D, rowB[:, :],
                                                              ALU.mult, ALU.subtract), inc="v")
            p.wait("scalar", "v", p.val("v"))
            p.emit("scalar", lambda e: e.activation(rowB[:, :], rowC[:, :], AF.Sqrt,
                                                    bias=cb[0:1, 0:1], scale=1.0), inc="s")
            s_end = p.val("s")
            p.wait("vector", "s", s_end)
            p.emit("vector", lambda e: e.reciprocal(rowC[:, :], rowB[:, :]), inc="v")
            p.wait("tensor", "v", p.val("v"))
            MM(p, PS[2][:, :], onesrf[:, :], rowA[:, :], True, True)
            MM(p, PS[3][:, :], onesrf[:, :], rowC[:, :], True, True, inc="t")
            t_end = p.val("t")
            p.wait("vector", "t", t_end)
            for dc in range(DCN):
                p.emit("vector", lambda e, o=ln_tmp[:, :],
                       i0=stream[:, dc * 1024 + half * 512: dc * 1024 + half * 512 + 512],
                       i1=PS[2][:, :]: e.tensor_tensor(o, i0, i1, ALU.subtract), inc="v")
                p.wait("vector", "v", p.val("v"))
                p.emit("vector", lambda e, o=dest(dc), i0=ln_tmp[:, :], i1=PS[3][:, :]:
                       e.tensor_tensor(o, i0, i1, ALU.mult), inc="v")
                p.wait("vector", "v", p.val("v"))
            return p.val("v"), t_end, s_end

        for o, i in ((ident, ident_i), (maskt, maskt_i), (onesb, onesb_i),
                     (onescf, onescf_i), (onesrf, onesrf_i), (cb, cb_i), (selr, selv_i)):
            DMA(p, "gpsimd", "dc0", o[:, :], i[:, :])
        v_consts = p.val("dc0")
        for dc in range(DCN):
            DMA(p, "gpsimd", "dx", stream[:, dc * 1024:(dc + 1) * 1024],
                xt[dc * 128:(dc + 1) * 128, :])
        v_xload = p.val("dx")

        chunks = []
        for r0 in (0, 128):
            chunks.append((0, w1t_sh, r0, 128, HID))
        for r0 in (0, 128):
            chunks.append((1, w2t_sh, r0, 128, HID))
        for r0 in range(0, 704, 128):
            chunks.append((2, w3t_sh, r0, min(128, 704 - r0), D))

        p.emit("vector", lambda e: e.memset(parts[64:128, 9:10], 0.0), inc="v")
        p.emit("vector", lambda e: e.memset(parts[:, 10:16], 0.0), inc="v")
        p.emit("vector", lambda e: e.memset(gsum[0:1, 3:8], 0.0), inc="v")
        red_done = []
        for slot, (mi, src, r0, nr, cw) in enumerate(chunks):
            b = slot % 2
            lane = "dpa" if b == 0 else "dpb"
            if slot >= 2:
                p.wait("sync", "v", red_done[slot - 2])
            v_ld = DMA(p, "sync", lane, pw[0:nr, b * 5632:b * 5632 + cw], src[r0:r0 + nr, :])
            p.wait("vector", lane, v_ld)
            p.emit("vector",
                   lambda e, o=parts[0:nr, slot:slot + 1], i=pw[0:nr, b * 5632:b * 5632 + cw]:
                   e.reduce_sum(o, i, AX.X, apply_absolute_value=True), inc="v")
            red_done.append(p.val("v"))
        p.wait("tensor", "dc0", v_consts)
        p.wait("tensor", "v", p.val("v"))
        MM(p, PS[0][0:1, 0:16], onescf[:, :], parts[:, :], True, True, inc="t")
        p.wait("vector", "t", p.val("t"))
        p.emit("vector", lambda e: e.reduce_sum(gsum[0:1, 0:1], PS[0][0:1, 0:2], AX.X), inc="v")
        p.emit("vector", lambda e: e.reduce_sum(gsum[0:1, 1:2], PS[0][0:1, 2:4], AX.X), inc="v")
        p.emit("vector", lambda e: e.reduce_sum(gsum[0:1, 2:3], PS[0][0:1, 4:10], AX.X), inc="v")
        v_gsum = p.val("v")
        p.wait("gpsimd", "v", v_gsum)
        v_gs = DMA(p, "gpsimd", "dgm", gsrc[:, :], gsum[:, :])
        p.wait("gpsimd", "dgm", v_gs)
        p.emit("gpsimd", lambda e: e.collective_compute(
            "AllReduce", ALU.add, replica_groups=[list(range(NCORES))],
            ins=[gsrc.ap().opt()], outs=[gdst.ap().opt()]), inc="car")
        p.wait("gpsimd", "car", 1)
        v_gam = DMA(p, "gpsimd", "dgm", gam[:, :], gdst[:, :])
        p.wait("tensor", "dx", v_xload)
        p.wait("tensor", "dc0", v_consts)
        p.wait("vector", "dx", v_xload)
        p.wait("vector", "dc0", v_consts)
        p.wait("scalar", "dx", v_xload)
        xdest0 = lambda dc: xn[:, dc * 512:(dc + 1) * 512]
        ln1_pre = emit_ln(0, xdest0)

        p.wait("scalar", "dgm", v_gam)
        p.wait("scalar", "dc0", v_consts)
        p.wait("scalar", "t", ln1_pre[1])
        p.emit("scalar", lambda e: e.activation(rowA[0:1, 0:8], gam[:, :], AF.Copy,
                                                bias=5e-9, scale=0.5 / NW), inc="s")
        p.emit("scalar", lambda e: e.activation(rowB[0:1, 0:8], gam[:, :], AF.Copy,
                                                bias=-5e-9, scale=-0.5 / NW), inc="s")
        p.wait("tensor", "s", p.val("s"))
        p.wait("tensor", "v", ln1_pre[0])
        MM(p, PS[1][:, 0:8], onesrf[:, :], rowA[0:1, 0:8], True, True)
        MM(p, PS[2][:, 0:8], onesrf[:, :], rowB[0:1, 0:8], True, True)
        MM(p, PS[3][:, 0:8], onesrf[:, :], selr[:, :], True, True, inc="t")
        p.wait("scalar", "t", p.val("t"))
        p.emit("scalar", lambda e: e.copy(thrp[:, :], PS[1][:, 0:8]), inc="s")
        p.emit("scalar", lambda e: e.copy(thrn[:, :], PS[2][:, 0:8]), inc="s")
        p.emit("scalar", lambda e: e.copy(selb[:, :], PS[3][:, 0:8]), inc="s")
        s_thr = p.val("s")


        slot = 0
        store_done = []
        use_done = []
        for (mi, src, r0, nr, cw) in chunks:
            b = slot % 2
            lane = "dpa" if b == 0 else "dpb"
            slane = "dsa" if b == 0 else "dsb"
            if slot >= 2:
                p.wait("sync", "v", use_done[slot - 2])
            else:
                p.wait("sync", "v", red_done[slot - 2])
            v_ld = DMA(p, "sync", lane, pw[0:nr, b * 5632:b * 5632 + cw], src[r0:r0 + nr, :])
            p.wait("vector", lane, v_ld)
            if slot == 0:
                p.wait("vector", "s", s_thr)
            p.emit("vector", lambda e, o=qt1[0:nr, 0:cw], i=pw[0:nr, b * 5632:b * 5632 + cw],
                   s1=thrp[0:nr, mi:mi + 1]: e.tensor_scalar(o, i, s1, None, ALU.is_ge), inc="v")
            p.emit("vector", lambda e, o=qt2[0:nr, 0:cw], i=pw[0:nr, b * 5632:b * 5632 + cw],
                   s1=thrn[0:nr, mi:mi + 1]: e.tensor_scalar(o, i, s1, None, ALU.is_ge), inc="v")
            use_done.append(p.val("v"))
            p.wait("vector", "v", p.val("v"))
            if slot >= 2:
                p.wait("vector", store_done[slot - 2][0], store_done[slot - 2][1])
            p.emit("vector", lambda e, o=qbt[0:nr, b * 5632:b * 5632 + cw],
                   i0=qt1[0:nr, 0:cw], i1=qt2[0:nr, 0:cw]:
                   e.scalar_tensor_tensor(o, i0, -1.0, i1, ALU.add, ALU.add), inc="v")
            p.wait("gpsimd", "v", p.val("v"))
            qdst = (q1loc, q2loc, q3loc)[mi][r0:r0 + nr, :]
            store_done.append((slane, DMA(p, "gpsimd", slane, qdst,
                                          qbt[0:nr, b * 5632:b * 5632 + cw])))
            slot += 1
            if slot in (2, 4, 10):
                csem, locb, outb = {2: ("cq1", q1loc, q1t), 4: ("cq2", q2loc, q2t),
                                    10: ("cq3", q3loc, q3t)}[slot]
                for sd in store_done[-2:]:
                    p.wait("gpsimd", sd[0], sd[1])
                p.emit("gpsimd", lambda e, i=locb, o=outb: e.collective_compute(
                    "AllGather", ALU.bypass, replica_groups=[list(range(NCORES))],
                    ins=[i.ap().opt()], outs=[o.ap().opt()]), inc=csem)
        v_prolog_ve = p.val("v")
        prolog_stores = [(la, p.val(la)) for la in ("dsa", "dsb")]
        for ci, csrc in enumerate((wqt_sh, wkt_sh, wvt_sh, wot_sh)):
            lane = "dla" if ci % 2 == 0 else "dlb"
            v_cast = DMA(p, "gpsimd", lane, watloc[:, ci * D:(ci + 1) * D], csrc[:, :])
        p.wait("gpsimd", "dla", p.val("dla"))
        p.wait("gpsimd", "dlb", p.val("dlb"))
        p.emit("gpsimd", lambda e: e.collective_compute(
            "AllGather", ALU.bypass, replica_groups=[list(range(NCORES))],
            ins=[watloc.ap().opt()], outs=[wat.ap().opt()]), inc="cwat")

        fs = {}

        ffn_no = [0]

        def emit_ffn(halves=(0, 1), hpasses=(0, 1), do_p2=True):
            ffn_no[0] += 1
            first_call = ffn_no[0] == 1
            xdest = lambda dc: xn[:, dc * 512:(dc + 1) * 512]
            for half in halves:
                if first_call and half == 0:
                    v_xn, _, s_ln = ln1_pre
                else:
                    v_xn, _, s_ln = emit_ln(half, xdest)
                p.wait("tensor", "v", v_xn)
                p.wait("tensor", "s", s_ln)
                for hpass in hpasses:
                    hids = list(range(hpass * 22, hpass * 22 + 22))
                    groups = [hids[i:i + 4] for i in range(0, 22, 4)]
                    for gi, grp in enumerate(groups):
                        buf = gi % 2
                        if first_call and half == 0 and hpass == 0 and gi == 0:
                            p.wait("sync", "cq1", 1)
                            p.wait("sync", "cq2", 1)
                        wlane = "dwa" if buf == 0 else "dwb"
                        if first_call and half == 0 and hpass == 0 and gi in (0, 1):
                            p.wait("sync", "v", v_prolog_ve)
                            for la, lv in prolog_stores:
                                p.wait("sync", la, lv)
                        p.wait("sync", "t", fs.get(("w12f", buf), 0))
                        gw = len(grp) * 128
                        for mat, qsrc in ((0, q1t), (1, q2t)):
                            base = (buf * 2 + mat) * 16 * 512
                            dst = w12[:, base:base + 16 * 512].rearrange(
                                "p (dc c) -> p dc c", dc=16)[:, :, 0:gw]
                            srcap = qsrc.rearrange("(dc p) h -> p dc h", p=128)[
                                :, :, grp[0] * 128:grp[0] * 128 + gw]
                            DMA(p, "sync", wlane, dst, srcap)
                        v_slab = p.val(wlane)
                        p.wait("tensor", wlane, v_slab)
                        for li, hc in enumerate(grp):
                            pb = (li % 2) * 2
                            p.wait("tensor", "s", fs.get(("p1s", pb), 0))
                            p.wait("tensor", "v", fs.get(("p1v", pb), 0))
                            for dc in range(DCN):
                                base = ((buf * 2 + 0) * 16 + dc) * 512 + li * 128
                                MM(p, PS[pb][:, :], w12[:, base:base + 128],
                                   xn[:, dc * 512:(dc + 1) * 512], dc == 0, dc == DCN - 1,
                                   inc=("t" if dc == DCN - 1 else None))
                            v_x1 = p.val("t")
                            for dc in range(DCN):
                                base = ((buf * 2 + 1) * 16 + dc) * 512 + li * 128
                                MM(p, PS[pb + 1][:, :], w12[:, base:base + 128],
                                   xn[:, dc * 512:(dc + 1) * 512], dc == 0, dc == DCN - 1,
                                   inc=("t" if dc == DCN - 1 else None))
                            v_x2 = p.val("t")
                            sb = li % 2
                            p.wait("scalar", "t", v_x1)
                            p.wait("scalar", "v", fs.get(("sig", sb), 0))
                            p.emit("scalar", lambda e, o=sigt[:, sb * 512:(sb + 1) * 512],
                                   i=PS[pb][:, :]: e.activation(o, i, AF.Sigmoid), inc="s")
                            fs[("p1s", pb)] = p.val("s")
                            hx = hc - hpass * 22
                            p.wait("vector", "t", v_x2)
                            p.wait("vector", "s", p.val("s"))
                            p.emit("vector", lambda e, o=hbuf[:, hx * 512:(hx + 1) * 512],
                                   i0=sigt[:, sb * 512:(sb + 1) * 512], i1=PS[pb + 1][:, :]:
                                   e.tensor_tensor(o, i0, i1, ALU.mult), inc="v")
                            fs[("p1v", pb)] = p.val("v")
                            fs[("sig", sb)] = p.val("v")
                        fs[("w12f", buf)] = p.val("t")
                    v_hready = p.val("v")
                    for dg in (range(8) if do_p2 else range(0)):
                        buf = dg % 2
                        if first_call and half == 0 and hpass == 0 and dg == 0:
                            p.wait("sync", "cq3", 1)
                        wlane = "d3a" if buf == 0 else "d3b"
                        if first_call and half == 0 and hpass == 0 and dg in (0, 1):
                            p.wait("sync", "v", v_prolog_ve)
                            for la, lv in prolog_stores:
                                p.wait("sync", la, lv)
                        p.wait("sync", "t", fs.get(("w3f", buf), 0))
                        dst = w3s[:, buf * 22 * 256:(buf + 1) * 22 * 256].rearrange(
                            "p (i c) -> p i c", i=22)
                        srcap = q3t.rearrange("(i p) d -> p i d", p=128)[
                            :, hids[0]:hids[0] + 22, dg * 256:(dg + 1) * 256]
                        DMA(p, "sync", wlane, dst, srcap)
                        v_slab = p.val(wlane)
                        p.wait("tensor", wlane, v_slab)
                        p.wait("tensor", "v", v_hready)
                        for dc2 in range(2):
                            ob = 4 + dc2
                            p.wait("tensor", "v", fs.get(("p2v", ob), 0))
                            for i in range(22):
                                base = (buf * 22 + i) * 256 + dc2 * 128
                                MM(p, PS[ob][:, :], w3s[:, base:base + 128],
                                   hbuf[:, i * 512:(i + 1) * 512], i == 0, i == 21,
                                   inc=("t" if i == 21 else None))
                            v_out = p.val("t")
                            dci = dg * 2 + dc2
                            p.wait("vector", "t", v_out)
                            p.emit("vector", lambda e,
                                   o=stream[:, dci * 1024 + half * 512:dci * 1024 + half * 512 + 512],
                                   i0=PS[ob][:, :]:
                                   e.scalar_tensor_tensor(o, i0, 0.5, o, ALU.mult, ALU.add),
                                   inc="v")
                            fs[("p2v", ob)] = p.val("v")
                        fs[("w3f", buf)] = p.val("t")

        def emit_attn():
            for half in range(2):
                dest = lambda dc, h=half: xn2[:, dc * 1024 + h * 512: dc * 1024 + h * 512 + 512]
                v_xn2, _, s_ln = emit_ln(half, dest)
            p.wait("tensor", "v", v_xn2)
            p.wait("tensor", "s", s_ln)

            t_pre_attn = p.val("t")
            for g in range(8):
                h0 = g * 2
                bufA, bufB = 0, 1
                if g == 0:
                    p.wait("sync", "cwat", 1)
                    p.wait("sync", "t", t_pre_attn)
                p.wait("sync", "t", fs.get(("wslf", bufA), 0))
                DMA(p, "sync", "dqa",
                    wsl[:, bufA * 16 * 256:(bufA + 1) * 16 * 256].rearrange(
                        "p (dc c) -> p dc c", dc=16),
                    wat.rearrange("(dc p) d -> p dc d", p=128)[
                        :, :, 0 * D + h0 * 128:0 * D + h0 * 128 + 256])
                v_wq = ("dqa", p.val("dqa"))
                p.wait("sync", "t", fs.get(("wslf", bufB), 0))
                DMA(p, "sync", "dqb",
                    wsl[:, bufB * 16 * 256:(bufB + 1) * 16 * 256].rearrange(
                        "p (dc c) -> p dc c", dc=16),
                    wat.rearrange("(dc p) d -> p dc d", p=128)[
                        :, :, 1 * D + h0 * 128:1 * D + h0 * 128 + 256])
                v_wk = ("dqb", p.val("dqb"))

                def proj(bufX, v_w, evac):
                    p.wait("tensor", v_w[0], v_w[1])
                    for hh in range(2):
                        for colh in range(2):
                            pb = colh
                            p.wait("tensor", "s", fs.get(("pA", pb), 0))
                            p.wait("tensor", "v", fs.get(("pT", pb), 0))
                            for dc in range(DCN):
                                base = (bufX * 16 + dc) * 256 + hh * 128
                                MM(p, PS[pb][:, :], wsl[:, base:base + 128],
                                   xn2[:, dc * 1024 + colh * 512:dc * 1024 + colh * 512 + 512],
                                   dc == 0, dc == DCN - 1, inc=("t" if dc == DCN - 1 else None))
                            p.wait("scalar", "t", p.val("t"))
                            evac(h0 + hh, hh, colh, PS[pb][:, :])
                            fs[("pA", pb)] = p.val("s")

                def transposes(slot, src_off, dst_off):
                    p.wait("tensor", "s", p.val("s"))
                    for h4 in range(2):
                        pb = h4
                        p.wait("tensor", "v", fs.get(("pT", pb), 0))
                        p.wait("tensor", "s", fs.get(("pA", pb), 0))
                        for t4 in range(4):
                            tb = h4 * 4 + t4
                            MM(p, PS[pb][:, t4 * 128:t4 * 128 + 128],
                               kv[:, src_off + tb * 128:src_off + tb * 128 + 128],
                               ident[:, :], True, True, inc=("t" if t4 == 3 else None))
                        v_tr = p.val("t")
                        p.wait("vector", "t", v_tr)
                        p.emit("vector", lambda e, o=kv[:, dst_off + h4 * 512:dst_off + h4 * 512 + 512],
                               i=PS[pb][:, :]: e.tensor_copy(o, i), inc="v")
                        fs[("pT", pb)] = p.val("v")

                proj(bufA, v_wq, lambda h, hh, colh, ps: (
                    p.emit("scalar", lambda e, o=qt(h, colh * 512, colh * 512 + 512), i=ps:
                           e.activation(o, i, AF.Copy, scale=QSCALE), inc="s")))
                fs[("wslf", bufA)] = p.val("t")
                p.wait("sync", "t", fs[("wslf", bufA)])
                DMA(p, "sync", "dqa",
                    wsl[:, bufA * 16 * 256:(bufA + 1) * 16 * 256].rearrange(
                        "p (dc c) -> p dc c", dc=16),
                    wat.rearrange("(dc p) d -> p dc d", p=128)[
                        :, :, 2 * D + h0 * 128:2 * D + h0 * 128 + 256])
                v_wv = ("dqa", p.val("dqa"))

                proj(bufB, v_wk, lambda h, hh, colh, ps: (
                    p.emit("scalar", lambda e, o=kv[:, (hh * 3 + 0) * 1024 + colh * 512:
                           (hh * 3 + 0) * 1024 + colh * 512 + 512], i=ps:
                           e.sign(o, i), inc="s")))
                for hh in range(2):
                    transposes(hh, (hh * 3 + 0) * 1024, (hh * 3 + 1) * 1024)
                fs[("wslf", bufB)] = p.val("t")

                proj(bufA, v_wv, lambda h, hh, colh, ps: (
                    p.emit("scalar", lambda e, o=kv[:, (hh * 3 + 2) * 1024 + colh * 512:
                           (hh * 3 + 2) * 1024 + colh * 512 + 512], i=ps:
                           e.sign(o, i), inc="s")))
                for hh in range(2):
                    transposes(hh, (hh * 3 + 2) * 1024, (hh * 3 + 2) * 1024)
                fs[("wslf", bufA)] = p.val("t")
                p.wait("tensor", "v", p.val("v"))
                p.wait("tensor", "s", p.val("s"))

                NB = 8
                for n in range(NB):
                    for hh in range(2):
                        h = h0 + hh
                        sT_b = PS[2 + hh]
                        out_b = PS[0 + hh]
                        sbank = 4 + (n % 2) * 2 + hh
                        kfm = kv[:, (hh * 3 + 0) * 1024 + n * 128:(hh * 3 + 0) * 1024 + n * 128 + 128]
                        kT = kv[:, (hh * 3 + 1) * 1024 + n * 128:(hh * 3 + 1) * 1024 + n * 128 + 128]
                        vT = kv[:, (hh * 3 + 2) * 1024 + n * 128:(hh * 3 + 2) * 1024 + n * 128 + 128]
                        qc = qt(h, n * 128, n * 128 + 128)
                        p.wait("tensor", "v", fs.get(("sT_free", hh), 0))
                        MM(p, sT_b[:, 0:128], kfm, qc, True, True, inc="t")
                        v_st = p.val("t")
                        p.wait("vector", "t", v_st)
                        p.wait("vector", "t", fs.get(("stsb_free", hh), 0))
                        p.emit("vector", lambda e, o=stsb2[hh][:, :], i0=sT_b[:, 0:128],
                               i1=maskt[:, :]: e.tensor_tensor(o, i0, i1, ALU.mult), inc="v")
                        fs[("sT_free", hh)] = p.val("v")
                        v_mask = p.val("v")
                        p.wait("tensor", "v", v_mask)
                        p.wait("tensor", "v", fs.get(("out_free", hh), 0))
                        p.wait("tensor", "v", fs.get(("sbf", hh), 0))
                        if n == 0:
                            MM(p, out_b[:, 0:128], vT, stsb2[hh][:, :], True, True, inc="t")
                        else:
                            MM(p, out_b[:, 0:128], vT, stsb2[hh][:, :], True, False)
                            MM(p, out_b[:, 0:128], state_bf[:, hh * 128:hh * 128 + 128],
                               qc, False, True, inc="t")
                        v_out = p.val("t")
                        fs[("stsb_free", hh)] = v_out
                        p.wait("vector", "t", v_out)
                        p.emit("vector", lambda e, o=ao[:, h * 1024 + n * 128:h * 1024 + n * 128 + 128],
                               i=out_b[:, 0:128]: e.tensor_copy(o, i), inc="v")
                        fs[("out_free", hh)] = p.val("v")
                        p.wait("tensor", "v", fs.get(("stev", sbank), 0))
                        MM(p, PS[sbank][:, 0:128], kT, vT, True, True, inc="t")
                        v_ckv = p.val("t")
                        p.wait("vector", "t", v_ckv)
                        stf = ln_tmp[:, hh * 128:hh * 128 + 128]
                        if n == 0:
                            p.emit("vector", lambda e, o=stf,
                                   i=PS[sbank][:, 0:128]: e.tensor_copy(o, i), inc="v")
                        else:
                            p.emit("vector", lambda e, o=stf, i0=stf,
                                   i1=PS[sbank][:, 0:128]:
                                   e.tensor_tensor(o, i0, i1, ALU.add), inc="v")
                        fs[("stev", sbank)] = p.val("v")
                        if n < NB - 1:
                            p.emit("vector", lambda e, o=state_bf[:, hh * 128:hh * 128 + 128],
                                   i=stf: e.tensor_copy(o, i), inc="v")
                        else:
                            p.emit("vector", lambda e, o=ltot[:, h * 128:h * 128 + 128],
                                   i=stf: e.tensor_copy(o, i), inc="v")
                        fs[("sbf", hh)] = p.val("v")
                fs[("pT", 0)] = max(fs[("pT", 0)], fs[("out_free", 0)])
                fs[("pT", 1)] = max(fs[("pT", 1)], fs[("out_free", 1)])
                if g == 3:
                    p.wait("gpsimd", "v", p.val("v"))
                    v_lo = DMA(p, "gpsimd", "dgm", ltoc_lo[:, :], ltot[:, 0:1024])
                    p.wait("gpsimd", "dgm", v_lo)
                    p.emit("gpsimd", lambda e: e.collective_compute(
                        "AllGather", ALU.bypass, replica_groups=[list(range(NCORES))],
                        ins=[ltoc_lo.ap().opt()], outs=[ltg_lo.ap().opt()]), inc="cltg")

            v_ltot = p.val("v")
            p.wait("gpsimd", "v", v_ltot)
            v_hi = DMA(p, "gpsimd", "dgm", ltoc_hi[:, :], ltot[:, 1024:2048])
            p.wait("gpsimd", "dgm", v_hi)
            p.emit("gpsimd", lambda e: e.collective_compute(
                "AllGather", ALU.bypass, replica_groups=[list(range(NCORES))],
                ins=[ltoc_hi.ap().opt()], outs=[ltg_hi.ap().opt()]), inc="cltg2")
            p.emit("vector", lambda e: e.memset(accI[:, :], 0.0), inc="v")
            p.wait("vector", "v", p.val("v"))
            p.wait("gpsimd", "cltg", 1)
            p.wait("gpsimd", "cltg2", 1)
            for part, ltgb in ((0, ltg_lo), (1, ltg_hi)):
                for j in range(NCORES):
                    b = j % 2
                    llane = "dla" if b == 0 else "dlb"
                    p.wait("gpsimd", "v", fs.get(("ltg", b), 0))
                    v_ld = DMA(p, "gpsimd", llane, ltg_t[:, b * 2048:b * 2048 + 1024],
                               ltgb[j * 128:(j + 1) * 128, :])
                    p.wait("vector", llane, v_ld)
                    p.emit("vector", lambda e, i0=ltg_t[:, b * 2048:b * 2048 + 1024],
                           s1=selb[:, j:j + 1],
                           o=accI[:, part * 1024:(part + 1) * 1024]:
                           e.scalar_tensor_tensor(o, i0, s1, o, ALU.mult, ALU.add), inc="v")
                    fs[("ltg", b)] = p.val("v")
                    p.wait("vector", "v", p.val("v"))
            p.emit("vector", lambda e: e.tensor_copy(init_bf[:, :], accI[:, :]), inc="v")
            v_init = p.val("v")
            p.wait("tensor", "v", v_init)
            for h in range(NH):
                for colh in range(2):
                    pb = colh
                    p.wait("tensor", "v", fs.get(("pI", pb), 0))
                    MM(p, PS[pb][:, :], init_bf[:, h * 128:(h + 1) * 128],
                       qt(h, colh * 512, colh * 512 + 512),
                       True, True, inc="t")
                    v_mmi = p.val("t")
                    p.wait("vector", "t", v_mmi)
                    p.emit("vector", lambda e, o=ao[:, h * 1024 + colh * 512:
                           h * 1024 + colh * 512 + 512], i0=PS[pb][:, :]:
                           e.tensor_tensor(o, o, i0, ALU.add), inc="v")
                    fs[("pI", pb)] = p.val("v")
            v_ao = p.val("v")
            p.wait("tensor", "v", v_ao)
            for half in range(2):
                for dg in range(8):
                    buf = dg % 2
                    if half == 0 and dg == 0:
                        p.wait("sync", "cwat", 1)
                        p.wait("sync", "v", v_ao)
                    wlane = "dqa" if buf == 0 else "dqb"
                    p.wait("sync", "t", fs.get(("wof", buf), 0))
                    DMA(p, "sync", wlane,
                        wosl[:, buf * 16 * 256:(buf + 1) * 16 * 256].rearrange(
                            "p (dc c) -> p dc c", dc=16),
                        wat.rearrange("(dc p) d -> p dc d", p=128)[
                            :, :, 3 * D + dg * 256:3 * D + dg * 256 + 256])
                    v_slab = p.val(wlane)
                    p.wait("tensor", wlane, v_slab)
                    for dc2 in range(2):
                        pb = dc2
                        p.wait("tensor", "v", fs.get(("pI", pb), 0))
                        for dc in range(DCN):
                            base = (buf * 16 + dc) * 256 + dc2 * 128
                            MM(p, PS[pb][:, :], wosl[:, base:base + 128],
                               ao[:, dc * 1024 + half * 512:dc * 1024 + half * 512 + 512],
                               dc == 0, dc == DCN - 1, inc=("t" if dc == DCN - 1 else None))
                        v_mmo = p.val("t")
                        dci = dg * 2 + dc2
                        p.wait("vector", "t", v_mmo)
                        p.emit("vector", lambda e,
                               o=stream[:, dci * 1024 + half * 512:dci * 1024 + half * 512 + 512],
                               i0=PS[pb][:, :]: e.tensor_tensor(o, o, i0, ALU.add), inc="v")
                        fs[("pI", pb)] = p.val("v")
                    fs[("wof", buf)] = p.val("t")

        if upto == "ln1":
            emit_ln(0, lambda dc: xn[:, dc * 512:(dc + 1) * 512])
        elif upto == "p1":
            emit_ffn(halves=(0,), hpasses=(0,), do_p2=False)
        elif upto == "p2":
            emit_ffn(halves=(0,), hpasses=(0,), do_p2=True)
        elif upto != "prolog":
            emit_ffn()
        if upto in ("attn", "all"):
            emit_attn()
        if upto == "all":
            p.wait("sync", "v", p.val("v"))
            for _pb in (0, 2):
                fs[("p1s", _pb)] = p.val("s")
                fs[("p1v", _pb)] = p.val("v")
            for _ob in (4, 5):
                fs[("p2v", _ob)] = p.val("v")
            emit_ffn()

        v_last = p.val("v")
        p.wait("gpsimd", "v", v_last)
        for dc in range(DCN):
            DMA(p, "gpsimd", "st", yt[dc * 128:(dc + 1) * 128, :],
                stream[:, dc * 1024:(dc + 1) * 1024])
        p.wait("gpsimd", "st", p.val("st"))

        block = es.enter_context(nc.Block())

        @block.sync
        def _(eng):
            for f in p.ops["sync"]:
                f(eng)

        @block.tensor
        def _(eng):
            for f in p.ops["tensor"]:
                f(eng)

        @block.vector
        def _(eng):
            for f in p.ops["vector"]:
                f(eng)

        @block.scalar
        def _(eng):
            for f in p.ops["scalar"]:
                f(eng)

        @block.gpsimd
        def _(eng):
            for f in p.ops["gpsimd"]:
                f(eng)

    return nc


def make_inputs(L0_attn_x, L0_attn_Wq, L0_attn_Wk, L0_attn_Wv, L0_attn_Wo,
                L0_ffn_w1, L0_ffn_w2, L0_ffn_w3):
    xf = np.asarray(L0_attn_x, np.float32).reshape(NCORES * TOK, D)
    w1t = np.ascontiguousarray(np.asarray(L0_ffn_w1, np.float32).T)
    w2t = np.ascontiguousarray(np.asarray(L0_ffn_w2, np.float32).T)
    w3t = np.ascontiguousarray(np.asarray(L0_ffn_w3, np.float32).T)
    wqt = np.ascontiguousarray(np.asarray(L0_attn_Wq, np.float32).T)
    wkt = np.ascontiguousarray(np.asarray(L0_attn_Wk, np.float32).T)
    wvt = np.ascontiguousarray(np.asarray(L0_attn_Wv, np.float32).T)
    wot = np.ascontiguousarray(np.asarray(L0_attn_Wo, np.float32).T)
    ident = np.eye(128, dtype=BFNP)
    maskt = np.triu(np.ones((128, 128), np.float32))
    onesb = np.ones((128, 1), BFNP)
    onescf = np.ones((128, 1), np.float32)
    onesrf = np.ones((1, 128), np.float32)
    cbv = np.broadcast_to(np.array([[1e-5, 5e-9, -5e-9, 0, 0, 0, 0, 0]], np.float32), (128, 8)).copy()
    ims = []
    for r in range(NCORES):
        sel = np.zeros((1, 8), np.float32)
        for j in range((r // 4) * 4, r):
            sel[0, j] = 1.0
        ims.append({
            "xt": np.ascontiguousarray(xf[r * TOK:(r + 1) * TOK, :].T),
            "w1t_sh": np.ascontiguousarray(w1t[r * 256:(r + 1) * 256, :]),
            "w2t_sh": np.ascontiguousarray(w2t[r * 256:(r + 1) * 256, :]),
            "w3t_sh": np.ascontiguousarray(w3t[r * 704:(r + 1) * 704, :]),
            "wqt_sh": np.ascontiguousarray(wqt[r * 256:(r + 1) * 256, :]),
            "wkt_sh": np.ascontiguousarray(wkt[r * 256:(r + 1) * 256, :]),
            "wvt_sh": np.ascontiguousarray(wvt[r * 256:(r + 1) * 256, :]),
            "wot_sh": np.ascontiguousarray(wot[r * 256:(r + 1) * 256, :]),
            "ident_i": ident, "maskt_i": maskt, "onesb_i": onesb,
            "onescf_i": onescf, "onesrf_i": onesrf, "cb_i": cbv, "selv_i": sel,
        })
    return ims


def run(upto="all", **inputs):
    nc = build(upto)
    ims = make_inputs(**inputs)
    res = run_bass_kernel_spmd(nc, ims, core_ids=list(range(NCORES)))
    outs = [np.asarray(res.results[r]["yt"]).T for r in range(NCORES)]
    return np.concatenate(outs, axis=0).reshape(2, 4096, 2048)


def kernel(**inputs):
    return run("all", **inputs)
